# version 18 (final)
# speedup vs baseline: 1.1520x; 1.0004x over previous
import numpy as np
import ml_dtypes
from contextlib import ExitStack

import concourse.bass as bass
import concourse.mybir as mybir
from concourse import library_config
from concourse.bass_utils import run_bass_kernel_spmd

F32 = mybir.dt.float32
BF16 = mybir.dt.bfloat16
AF = mybir.ActivationFunctionType
ALU = mybir.AluOpType
AX = mybir.AxisListType
BF = ml_dtypes.bfloat16

C = 1024
NCH = 8
HF = 4096
NSSM = 16
NH = 16
HD = 64
LN_EPS = 1e-5
GN_EPS = 0.00064
RMS_EPS = 1e-6
QEPS = 1e-5

TCM = 512
NTT = TCM // 128
TCW = 256
NQ = TCW // 128
NWC = TCM // TCW

STAGES = "ABFW24"
DEBUG = False
A_BCDT = True
A_XLN = 2
F_PAIRS = 2
A_NLIM = 16
A_SCAN = True
A_TAIL = True

TE, VE, SC, GP, SY = "TE", "VE", "SC", "GP", "SY"
QS, QA, QG = "QS", "QA", "QG"
QENG = {QS: SY, QA: SC, QG: GP}
ESTREAMS = [TE, VE, SC, GP, SY]


class Buf:
    __slots__ = ("name", "w", "readers")

    def __init__(self, name):
        self.name = name
        self.w = None
        self.readers = []


class Plan:
    def __init__(self, nc, sems, sem_factory):
        self.nc = nc
        self.sems = sems
        self.sem_factory = sem_factory
        self.thunks = {e: [] for e in ESTREAMS}
        self.cnt = {k: 0 for k in (TE, VE, SC, GP)}
        self.scnt = {}
        self.wm = {e: {} for e in ESTREAMS}
        self.own_cover = {e: 0 for e in ESTREAMS}
        self.nops = 0
        self.bufs = {}

    def B(self, name):
        if name not in self.bufs:
            self.bufs[name] = Buf(name)
        return self.bufs[name]

    def slot_sem(self, slot):
        if slot not in self.sems:
            self.sems[slot] = self.sem_factory("d_" + slot)
            self.scnt[slot] = 0
        return self.sems[slot]

    def task(self, eng):
        return Task(self, eng)

    def dma(self, q, out, in_, reads=(), writes=(), slot=None):
        stream = QENG[q]
        if slot is None:
            slot = writes[0].name if writes else "o_" + reads[0].name
        sem = self.slot_sem(slot)
        deps = []
        for b in reads:
            if b.w is not None:
                deps.append(b.w)
        for b in writes:
            if b.w is not None:
                deps.append(b.w)
            deps.extend(b.readers)
        for key, val in deps:
            if key == stream:
                if self.own_cover[stream] < val:
                    self.thunks[stream].append(lambda e: e.drain())
                    self.own_cover[stream] = self.cnt.get(stream, 0)
            elif self.wm[stream].get(key, 0) < val:
                s2 = self.sems[key]
                self.thunks[stream].append(
                    lambda e, s=s2, v=val: e.wait_ge(s, v))
                self.wm[stream][key] = val
        self.scnt[slot] += 16
        tgt = self.scnt[slot]

        def f(e, out=out, in_=in_, sem=sem):
            e.dma_start(out=out, in_=in_).then_inc(sem, 16)
        self.thunks[stream].append(f)
        self.nops += 1
        for b in reads:
            b.readers.append((slot, tgt))
        for b in writes:
            b.w = (slot, tgt)
            b.readers = []


class Task:
    def __init__(self, plan, eng):
        self.p = plan
        self.eng = eng
        self.prog = []
        self.written = set()
        self.undrained = set()
        self.reads = []
        self.writes = []

    def __enter__(self):
        return self

    def op(self, emit_fn, reads=(), writes=(), waw=True):
        p = self.p
        pend = []
        for b in reads:
            if b not in self.written and b.w is not None:
                pend.append(b.w)
        for b in writes:
            if b not in self.written:
                if waw and b.w is not None:
                    pend.append(b.w)
                pend.extend(b.readers)
        need_drain = any(b in self.undrained for b in reads) or (
            self.eng != TE and any(b in self.undrained for b in writes))
        waits = []
        for key, val in pend:
            if key == self.eng:
                if p.own_cover[self.eng] < val:
                    need_drain = True
            elif p.wm[self.eng].get(key, 0) < val:
                waits.append((p.sems[key], val))
                p.wm[self.eng][key] = val
        if need_drain:
            self.prog.append(("drain",))
            self.undrained.clear()
            p.own_cover[self.eng] = p.cnt[self.eng]
        for s, v in waits:
            self.prog.append(("wait", s, v))
        self.prog.append(("emit", emit_fn))
        p.nops += 1
        for b in writes:
            self.written.add(b)
            self.undrained.add(b)
        self.reads.extend(reads)
        self.writes.extend(writes)

    def __exit__(self, et, ev, tb):
        if et is not None:
            return False
        p = self.p
        if not any(o[0] == "emit" for o in self.prog):
            return False
        p.cnt[self.eng] += 1
        tgt = p.cnt[self.eng]
        sem = p.sems[self.eng]
        prog = self.prog
        last = max(i for i, o in enumerate(prog) if o[0] == "emit")

        def body(e, prog=prog, last=last, sem=sem):
            for i, o in enumerate(prog):
                if o[0] == "wait":
                    e.wait_ge(o[1], o[2])
                elif o[0] == "drain":
                    e.drain()
                else:
                    inst = o[1](e)
                    if i == last:
                        inst.then_inc(sem, 1)
        p.thunks[self.eng].append(body)
        for b in self.reads:
            b.readers.append((self.eng, tgt))
        for b in self.writes:
            b.w = (self.eng, tgt)
            b.readers = []
        return False


def build_program(T):
    NW = T // TCM
    nc = bass.Bass("TRN2", target_bir_lowering=False, debug=DEBUG)

    def DI(name, shape):
        return nc.dram_tensor(name, list(shape), F32, kind="ExternalInput").ap()

    x_in = DI("x", (T, C))
    y_out = nc.dram_tensor("y", [T, C], F32, kind="ExternalOutput").ap()
    conv_k = DI("conv_kernel", (3, 1, C))
    A_log = DI("A_log", (C, NSSM))
    W_bc = DI("W_bc", (C, 2 * NSSM))
    b_bc = DI("b_bc", (2 * NSSM,))
    Dcoef = DI("Dcoeff", (C,))
    W_dt = DI("W_dt", (C, 64))
    b_dt = DI("b_dt", (64,))
    W_dtp = DI("W_dtp", (64, C))
    b_dtp = DI("b_dtp", (C,))
    maa_x = DI("time_maa_x", (C,))
    maa_w = DI("time_maa_w", (C,))
    maa_k = DI("time_maa_k", (C,))
    maa_v = DI("time_maa_v", (C,))
    maa_r = DI("time_maa_r", (C,))
    maa_w1 = DI("time_maa_w1", (C, 128))
    maa_w2 = DI("time_maa_w2", (4, 32, C))
    t_decay = DI("time_decay", (C,))
    dec_w1 = DI("time_decay_w1", (C, 64))
    dec_w2 = DI("time_decay_w2", (64, C))
    faaaa = DI("time_faaaa", (NH, HD))
    W_r = DI("W_r", (C, C))
    W_k = DI("W_k", (C, C))
    W_v = DI("W_v", (C, C))
    W_o = DI("W_o", (C, C))
    ln_w = DI("ln_w", (C,))
    ln_b = DI("ln_b", (C,))
    ffn_w1 = DI("ffn_w1", (HF, C))
    ffn_b1 = DI("ffn_b1", (HF,))
    ffn_w2 = DI("ffn_w2", (C, HF))
    ffn_b2 = DI("ffn_b2", (C,))

    w1T_d = nc.dram_tensor("w1T_d", [C, HF], BF16, kind="Internal").ap()
    w2T_d = nc.dram_tensor("w2T_d", [HF, C], BF16, kind="Internal").ap()
    Wr_d = nc.dram_tensor("Wr_d", [C, C], BF16, kind="Internal").ap()
    Wk_d = nc.dram_tensor("Wk_d", [C, C], BF16, kind="Internal").ap()
    Wv_d = nc.dram_tensor("Wv_d", [C, C], BF16, kind="Internal").ap()
    Wo_d = nc.dram_tensor("Wo_d", [C, C], BF16, kind="Internal").ap()
    b1_d = nc.dram_tensor("b1_d", [1, HF], BF16, kind="Internal").ap()
    b2_d = nc.dram_tensor("b2_d", [1, C], BF16, kind="Internal").ap()
    mw2_d = nc.dram_tensor("mw2_d", [32, 4 * C], BF16, kind="Internal").ap()

    ident16_c = nc.inline_tensor(np.eye(128, dtype=BF), "ident16").ap()
    ident32_c = nc.inline_tensor(np.eye(128, dtype=np.float32), "ident32").ap()
    maskSU_c = nc.inline_tensor(
        np.triu(np.ones((128, 128), np.float32), 1).astype(BF), "maskSU").ap()
    onesrow_c = nc.inline_tensor(np.ones((1, 128), dtype=BF), "onesrow").ap()
    onesrow32_c = nc.inline_tensor(np.ones((1, 128), dtype=np.float32),
                                   "onesrow32").ap()
    onescol_c = nc.inline_tensor(np.ones((128, 1), dtype=np.float32),
                                 "onescol").ap()
    ind = np.zeros((NH, NCH * 128), np.float32)
    for cch in range(NCH):
        for p_ in range(128):
            ind[2 * cch + p_ // 64, cch * 128 + p_] = 1.0
    CONSTVALS = [LN_EPS, RMS_EPS, GN_EPS, 1.0, -0.5, -1.0, 1.0 / C,
                 1.0 / HF]
    cconst_c = nc.inline_tensor(
        np.tile(np.array(CONSTVALS, np.float32), (128, 1)), "cconsts").ap()
    gnind_c = nc.inline_tensor(ind.astype(BF), "gnind").ap()
    su = np.zeros((128, NCH * NH), np.float32)
    for cch in range(NCH):
        for half in range(2):
            su[half * 64:(half + 1) * 64, cch * NH + 2 * cch + half] = 1.0
    gnsum_c = nc.inline_tensor(su.astype(BF), "gnsum").ap()

    stack = ExitStack()
    with stack:
        def sb(name, shape, dt):
            return stack.enter_context(nc.sbuf_tensor(name, list(shape), dt))

        sems = {}
        for k in (TE, VE, SC, GP):
            sems[k] = stack.enter_context(nc.semaphore("sem_" + k))

        def sem_factory(name):
            return stack.enter_context(nc.semaphore(name))
        P = Plan(nc, sems, sem_factory)
        B = P.B
        ST = B("ST")

        pY = stack.enter_context(nc.psum_tensor([128, 512], F32))
        pM = stack.enter_context(nc.psum_tensor([128, 512], F32))
        pN = stack.enter_context(nc.psum_tensor([128, 512], F32))
        pB_ = stack.enter_context(nc.psum_tensor([128, 512], F32))
        pF = stack.enter_context(nc.psum_tensor([128, 1024], F32))
        pO = stack.enter_context(nc.psum_tensor([128, 1024], F32))
        pMb = pM.bitcast(mybir.dt.bfloat16)
        pNb = pN.bitcast(mybir.dt.bfloat16)

        bY = B("bankY")
        bM = B("bankM")
        bN = B("bankN")
        bB = B("bankB")
        bF0 = B("bankF0")
        bF1 = B("bankF1")
        bO0 = B("bankO0")
        bO1 = B("bankO1")
        bpY = bY
        bpM = [bM] * 4
        bpN = bN
        bpB = [bB] * 4
        bpF = [bF0, bF0, bF1, bF1]
        bpO = [bO0, bO0, bO1, bO1]
        pM_ALL = [bM]
        pB_ALL = [bB]
        ALL_pF = [bF0, bF1]
        ALL_pO = [bO0, bO1]

        s_ident = sb("c_ident", [128, 128], BF16)
        s_id32 = sb("c_id32", [128, 128], F32)
        s_mask = sb("c_mask", [128, 128], BF16)
        s_ones = sb("c_ones", [1, 128], BF16)
        s_ones32 = sb("c_ones32", [1, 128], F32)
        s_const = sb("c_const", [128, 8], F32)
        s_lnb16 = sb("c_lnb16", [128, 1], BF16)
        s_onesc = sb("c_onesc", [128, 1], F32)
        s_ones1 = sb("c_ones1", [128, 128], BF16)
        s_gnind = sb("c_gnind", [NH, NCH * 128], BF16)
        s_gnsum = sb("c_gnsum", [128, NCH * NH], BF16)

        s_Acoef = sb("w_Acoef", [128, NCH * NSSM], F32)
        s_convk = sb("w_convk", [128, 3 * NCH], F32)
        VEC_NAMES = ["maax", "td", "ntd", "D", "bdtp", "lnw", "lnb"]
        s_vec = sb("w_vec", [128, len(VEC_NAMES) * NCH], F32)
        VEC = {n: i for i, n in enumerate(VEC_NAMES)}

        def vcol(n, cch):
            i = VEC[n] * NCH + cch
            return s_vec[:, i:i + 1]

        s_maacol = sb("w_maacol", [128, 4 * NCH], F32)

        def maacol(f, cch):
            return s_maacol[:, f * NCH + cch:f * NCH + cch + 1]

        s_bbc = sb("w_bbc", [32, 1], F32)
        s_bdt = sb("w_bdt", [64, 1], F32)
        s_b1s = [sb(f"w_b1s{i}", [1, 512], BF16) for i in range(2)]
        s_lnbWo = sb("w_lnbWo", [1, C], BF16)
        s_ub = sb("w_ub", [128, NCH * NH], BF16)
        s_Wbc = sb("w_Wbc", [128, NCH * 32], BF16)
        s_Wdt = sb("w_Wdt", [128, NCH * 64], BF16)
        s_Wdtp = sb("w_Wdtp", [64, C], BF16)
        s_mw1 = sb("w_mw1", [128, NCH * 128], BF16)
        s_mw2s = [sb(f"w_mw2s{i}", [32, C], BF16) for i in range(2)]
        s_dw1 = sb("w_dw1", [128, NCH * 64], BF16)
        s_dw2 = sb("w_dw2", [64, C], BF16)
        s_wqs = sb("w_wqs", [128, 4], F32)

        s_w1r = [sb(f"st_w1r{i}", [128, 512], BF16) for i in range(8)]
        NW1R = 8
        s_w2s = [sb(f"st_w2s{i}", [128, 512], BF16) for i in range(4)]
        s_wrk = [sb(f"st_wrk{i}", [128, 512], BF16) for i in range(2)]
        s_wvo = [sb(f"st_wvo{i}", [128, 512], BF16) for i in range(4)]

        XRR = 7
        s_xres = [sb(f"a_xres{i}", [128, C], F32) for i in range(XRR)]
        s_xtok = [sb(f"a_xtok{i}", [128, C], BF16) for i in range(2)]
        s_stat = sb("a_stat", [128, 128], F32)
        s_bn = sb("a_bn", [128, 14], F32)
        s_scb = sb("a_scb", [1, 4 * 128], BF16)

        XF_W = TCM + 2
        s_xlnf = sb("a_xlnf", [128, NCH * XF_W], BF16)

        def xlnf(cch):
            return s_xlnf[:, cch * XF_W:(cch + 1) * XF_W]

        s_u = None
        s_dt = sb("a_dt0", [128, TCM], BF16)
        s_xdt = [sb(f"a_xdt{i}", [128, TCM], BF16) for i in range(2)]
        s_al = [sb(f"a_al{i}", [128, TCM], BF16) for i in range(2)]
        s_be = [sb(f"a_be{i}", [128, TCM], BF16) for i in range(2)]
        s_hh = [sb(f"a_h{i}", [128, TCM], BF16) for i in range(2)]
        s_chh = [sb(f"a_ch{i}", [128, TCM], BF16) for i in range(2)]
        s_bc16 = sb("a_bc16", [32, TCM], BF16)
        s_Brep = sb("a_Brep", [128, NSSM * TCM], BF16)
        s_Crep = sb("a_Crep", [128, NSSM * TCM], BF16)
        s_bcr = sb("a_bcr", [1, 2 * TCM], BF16)
        s_dtpj = sb("a_dtpj", [64, TCM], BF16)
        s_dec = s_dtpj[0:64, 0:TCW]
        s_e1 = sb("a_e1", [128, TCM], F32)
        e1_i32 = s_e1.bitcast(mybir.dt.int32)
        s_state = sb("a_state", [128, 128], F32)
        s_yf = sb("a_yf", [128, TCM], BF16)
        s_cvt = sb("a_cvt", [128, TCM], BF16)
        s_u = s_cvt

        s_h16 = [sb(f"a_h16{i}", [128, HF], BF16) for i in range(2)]
        s_zq = sb("a_zq", [128, 512], F32)
        zq_i32 = s_zq.bitcast(mybir.dt.int32)
        s_rn2 = sb("a_rn2", [128, 512], F32)
        rn2_bf = s_rn2.bitcast(BF16)
        s_xq2t = sb("a_xq2t", [128, 4 * 256], BF16)

        def xq2t_slot(hc, kk):
            return s_xq2t[:, (hc % 4) * 256 + kk * 128:
                          (hc % 4) * 256 + (kk + 1) * 128]

        s_xq1f = [sb(f"a_xq1f{i}", [128, NCH * 128], BF16) for i in range(2)]

        X3_W = TCW + 1
        s_x3f = sb("a_x3f", [128, NCH * X3_W], BF16)

        def x3f(cch):
            return s_x3f[:, cch * X3_W:(cch + 1) * X3_W]

        s_xx = sb("a_xx", [128, NCH * TCW], BF16)
        s_mix = sb("a_mix0", [128, NCH * TCW], BF16)
        s_m1 = sb("a_m1", [32, 4 * TCW], BF16)
        s_rt = sb("a_rt", [128, NCH * TCW], BF16)
        s_kt = sb("a_kt", [128, NCH * TCW], BF16)
        s_rk = sb("a_rk", [128, TCW], BF16)
        s_EE = sb("a_E", [128, TCW], F32)
        s_frw = sb("a_frw", [128, NCH * TCW], BF16)
        s_fkw = sb("a_fkw", [128, NCH * TCW], BF16)
        s_eQw = sb("a_eQw", [128, NCH * NQ], F32)
        s_emw = sb("a_emw", [128, NCH * TCW], BF16)
        s_khT = sb("a_khT", [128, 2 * HD], BF16)
        s_AT = sb("a_AT", [128, 2 * 128], BF16)
        s_vtok = [sb(f"a_vt{i}", [128, C], BF16) for i in range(2)]
        s_ytok = sb("a_yt0", [128, C], BF16)
        s_S = sb("a_S", [128, NH // 2 * HD], F32)
        s_S16 = sb("a_S16", [128, NH // 2 * HD], BF16)
        s_g16 = s_h16[1].bitcast(F32)[0:NH, 1024:1024 + 2 * TCW]
        s_gbc = s_h16[1][0:NH, 3072:3072 + 2 * TCW]
        s_wt1 = sb("a_wt1", [128, TCW], F32)
        s_wyn = sb("a_wyn", [128, TCW], BF16)
        s_wxx = sb("a_wxx", [128, TCW], BF16)

        _statmap = {}

        def stcol(purpose):
            if purpose not in _statmap:
                _statmap[purpose] = len(_statmap)
                assert len(_statmap) <= 88, "stat pool exhausted"
            i = _statmap[purpose]
            return s_stat[:, i:i + 1]

        def xslot(tile):
            return tile % XRR

        def xrb(tile):
            return B(f"xres{xslot(tile)}")

        bRNX = B("t_rnx")
        bZQ = B("t_zq")
        bRN2 = B("t_rn2")
        bH3e = B("h163")
        bH3t = B("h163_tern")
        bH3s = B("h163_st")

        def prep():
            P.dma(QS, s_ident[:], ident16_c[:], writes=[ST])
            P.dma(QS, s_id32[:], ident32_c[:], writes=[ST])
            P.dma(QS, s_mask[:], maskSU_c[:], writes=[ST])
            P.dma(QS, s_ones[:], onesrow_c[:], writes=[ST])
            P.dma(QS, s_ones32[:], onesrow32_c[:], writes=[ST])
            P.dma(QS, s_const[:], cconst_c[:], writes=[ST])
            for _i, _v in enumerate(CONSTVALS):
                nc.const_aps.register(F32, _v, s_const[:, _i:_i + 1])
            P.dma(QS, s_onesc[:], onescol_c[:], writes=[ST])
            P.dma(QS, s_gnind[:], gnind_c[:], writes=[ST])
            P.dma(QS, s_gnsum[:], gnsum_c[:], writes=[ST])
            for cch in range(NCH):
                sl = slice(cch * 128, (cch + 1) * 128)
                P.dma(QS, s_Acoef[:, cch * NSSM:(cch + 1) * NSSM], A_log[sl, :],
                      writes=[ST])
                for j in range(3):
                    P.dma(QS, s_convk[:, 3 * cch + j:3 * cch + j + 1],
                          conv_k[j, 0, sl].rearrange("a -> a ()"), writes=[ST])
                for nm, src in (("maax", maa_x), ("td", t_decay), ("D", Dcoef),
                                ("bdtp", b_dtp), ("lnw", ln_w), ("lnb", ln_b)):
                    P.dma(QS, vcol(nm, cch), src[sl].rearrange("a -> a ()"),
                          writes=[ST])
                for f, src in enumerate((maa_w, maa_k, maa_v, maa_r)):
                    P.dma(QS, maacol(f, cch), src[sl].rearrange("a -> a ()"),
                          writes=[ST])
            P.dma(QS, s_bbc[:], b_bc.rearrange("a -> a ()"), writes=[ST])
            P.dma(QS, s_bdt[:], b_dt.rearrange("a -> a ()"), writes=[ST])
            _stv = P.scnt["ST"]
            for _eng in (TE, VE, SC, GP, SY):
                P.thunks[_eng].append(
                    lambda e, s=P.sems["ST"], v=_stv: e.wait_ge(s, v))
                P.wm[_eng]["ST"] = _stv
            with P.task(VE) as t:
                t.op(lambda e: e.memset(s_ones1[:], 1.0), writes=[ST])
            with P.task(VE) as t:
                for cch in range(NCH):
                    t.op(lambda e, cch=cch: e.tensor_scalar_mul(
                        vcol("ntd", cch), vcol("td", cch), -1.0),
                        reads=[ST], writes=[ST])
            with P.task(SC) as t:
                t.op(lambda e: e.activation(s_zq[:, 0:NCH * NSSM],
                                            s_Acoef[:, 0:NCH * NSSM], AF.Exp),
                     reads=[ST], writes=[bZQ])
            with P.task(VE) as t:
                t.op(lambda e: e.tensor_scalar_mul(s_Acoef[:, 0:NCH * NSSM],
                                                   s_zq[:, 0:NCH * NSSM],
                                                   -1.0),
                     reads=[bZQ], writes=[ST])

            def cast_load(dst, src, rows, cols):
                P.dma(QG, dst[:, :], src[:, :], writes=[ST], slot="castld")

            for cch in range(NCH):
                sl = slice(cch * 128, (cch + 1) * 128)
                cast_load(s_Wbc[:, cch * 32:(cch + 1) * 32], W_bc[sl, :],
                          128, 32)
                cast_load(s_Wdt[:, cch * 64:(cch + 1) * 64], W_dt[sl, :],
                          128, 64)
                cast_load(s_mw1[:, cch * 128:(cch + 1) * 128], maa_w1[sl, :],
                          128, 128)
                cast_load(s_dw1[:, cch * 64:(cch + 1) * 64], dec_w1[sl, :],
                          128, 64)
            cast_load(s_Wdtp[:], W_dtp[:, :], 64, C)
            cast_load(s_dw2[:], dec_w2[:, :], 64, C)
            for f in range(4):
                P.dma(QG, mw2_d[:, f * C:(f + 1) * C], maa_w2[f, :, :],
                      writes=[B("mw2d")], slot="mw2d")
            P.dma(QG, b1_d[0:1, :], ffn_b1.rearrange("a -> () a"),
                  writes=[B("b1d")], slot="b1d")
            P.dma(QG, b2_d[0:1, :], ffn_b2.rearrange("a -> () a"),
                  writes=[B("b2d")], slot="b2d")

            P.dma(QS, s_zq[0:NH, 0:HD], faaaa[:, :], writes=[bZQ])
            with P.task(TE) as t:
                t.op(lambda e: e.transpose(
                    pN[0:HD, 0:NH], s_zq[0:NH, 0:HD],
                    s_id32[0:NH, 0:NH]),
                    reads=[bZQ, ST], writes=[bpN])
            with P.task(VE) as t:
                t.op(lambda e: e.memset(s_ub[:], 0.0), writes=[ST])
            with P.task(VE) as t:
                for h in range(NH):
                    cch, half = h // 2, h % 2
                    dst = s_ub[half * 64:(half + 1) * 64,
                               cch * NH + h:cch * NH + h + 1]
                    src = pN[0:64, h:h + 1]
                    t.op(lambda e, dst=dst, src=src: e.tensor_copy(dst, src),
                         reads=[bpN], writes=[ST])

            def wquant(dst_d, w_src, rows_total, cols_total, wslot, name):
                ntile = rows_total // 128
                npiece = cols_total // 512
                acc = stcol(name + "_acc")
                par2 = (stcol(name + "_par0"), stcol(name + "_par1"))
                accb = B(name + "_accb")
                SRCS = (s_zq[:, 0:512], s_rn2[:, 0:512])
                SRCB = (bZQ, bRN2)
                first = True
                for c0, (i, kp) in enumerate(
                        (i, kp) for i in range(ntile)
                        for kp in range(npiece)):
                    s0 = c0 % 2
                    P.dma(QS, SRCS[s0],
                          w_src[i * 128:(i + 1) * 128,
                                kp * 512:(kp + 1) * 512], writes=[SRCB[s0]])
                    with P.task(SC) as t:
                        t.op(lambda e, s0=s0: e.activation(
                            s_e1[:, 0:512], SRCS[s0], AF.Abs,
                            accum_out=par2[s0]),
                            reads=[SRCB[s0]],
                            writes=[B(f"pv_wpar{s0}"), B("t_e1")])
                    with P.task(VE) as t:
                        if first:
                            t.op(lambda e, acc=acc, s0=s0: e.tensor_copy(
                                acc, par2[s0]),
                                reads=[B(f"pv_wpar{s0}")], writes=[accb])
                            first = False
                        else:
                            t.op(lambda e, acc=acc, s0=s0: e.tensor_add(
                                acc, acc, par2[s0]),
                                reads=[B(f"pv_wpar{s0}"), accb],
                                writes=[accb])
                with P.task(TE) as t:
                    t.op(lambda e, acc=acc: e.matmul(
                        pM[0:1, 0:1], acc, s_onesc[:], start=True, stop=True),
                        reads=[accb, ST], writes=[bpM[0]])
                with P.task(VE) as t:
                    t.op(lambda e: e.tensor_scalar(
                        s_stat[0:1, 126:127], pM[0:1, 0:1],
                        1.0 / (rows_total * cols_total), QEPS,
                        ALU.mult, ALU.max),
                        reads=[bpM[0]], writes=[B("pv_ws")])
                    t.op(lambda e: e.reciprocal(s_stat[0:1, 127:128],
                                                s_stat[0:1, 126:127]),
                         reads=[B("pv_ws")], writes=[B("pv_ws")])
                with P.task(TE) as t:
                    t.op(lambda e: e.matmul(pM[:, 2:3], s_ones32[0:1, :],
                                            s_stat[0:1, 127:128],
                                            start=True, stop=True),
                         reads=[B("pv_ws"), ST], writes=[bpM[0]])
                    t.op(lambda e: e.matmul(pM[:, 3:4], s_ones32[0:1, :],
                                            s_stat[0:1, 126:127],
                                            start=True, stop=True),
                         reads=[B("pv_ws"), ST], writes=[bpM[0]], waw=False)
                with P.task(VE) as t:
                    t.op(lambda e, wslot=wslot: e.tensor_copy(
                        s_wqs[:, wslot:wslot + 1], pM[:, 2:3]),
                        reads=[bpM[0]], writes=[ST])
                    t.op(lambda e, wslot=wslot: e.tensor_copy(
                        s_wqs[:, 2 + wslot:3 + wslot], pM[:, 3:4]),
                        reads=[bpM[0]], writes=[ST])
                WDST = (s_cvt[:, 0:512], s_yf[:, 0:512])
                WDSTB = (B("a_cvt"), B("yf"))
                WSRC = (s_xres[5][:, 0:512], s_xres[6][:, 0:512])
                WSRCB = (B("xres5"), B("xres6"))
                chunks = [(i, kp) for i in range(ntile)
                          for kp in range(npiece)]
                nck = len(chunks)
                for c in range(nck + 1):
                    s = c % 2
                    if c < nck:
                        i, kp = chunks[c]
                        P.dma(QS, WSRC[s],
                              w_src[i * 128:(i + 1) * 128,
                                    kp * 512:(kp + 1) * 512],
                              writes=[WSRCB[s]], slot=f"wqs{s}")
                        rnd_prod(s, WSRC[s], [WSRCB[s], ST],
                                 s_wqs[:, wslot:wslot + 1], [])
                        rnd_cast(s)
                    if c > 0:
                        sp = (c - 1) % 2
                        i, kp = chunks[c - 1]
                        rnd_tail(sp, WSRC[sp], [WSRCB[sp], ST],
                                 s_wqs[:, wslot:wslot + 1], [],
                                 WDST[sp], WDSTB[sp], clip=True)
                        for j in range(4):
                            ptile = pMb if j % 2 == 0 else pNb
                            ptb = bM if j % 2 == 0 else bN
                            with P.task(TE) as t:
                                t.op(lambda e, j=j, ptile=ptile, sp=sp:
                                     e.transpose(
                                         ptile[:, 0:128],
                                         WDST[sp][:, j * 128:(j + 1) * 128],
                                         s_ident[:]),
                                     reads=[WDSTB[sp], ST], writes=[ptb])
                            with P.task(VE if j % 2 else SC) as t:
                                fn = (lambda e, j=j, ptile=ptile:
                                      e.tensor_copy(
                                          s_h16[1][:, j * 128:
                                                   (j + 1) * 128],
                                          ptile[:, 0:128])) if j % 2 else (
                                    lambda e, j=j, ptile=ptile:
                                    e.copy(
                                        s_h16[1][:, j * 128:(j + 1) * 128],
                                        ptile[:, 0:128]))
                                t.op(fn, reads=[ptb], writes=[bH3e])
                        P.dma(QS,
                              dst_d[kp * 512:(kp + 1) * 512,
                                    i * 128:(i + 1) * 128]
                              .rearrange("(j p) c -> p j c", p=128),
                              s_h16[1][:, 0:512]
                              .rearrange("p (j c) -> p j c", j=4),
                              reads=[bH3e])

            wquant(w1T_d, ffn_w1, HF, C, 0, "w1")
            wquant(w2T_d, ffn_w2, C, HF, 1, "w2")

            for nm, wsrc, dst in (("wr", W_r, Wr_d), ("wk", W_k, Wk_d),
                                  ("wv", W_v, Wv_d), ("wo", W_o, Wo_d)):
                for i in range(NCH):
                    P.dma(QG, dst[i * 128:(i + 1) * 128, :],
                          wsrc[i * 128:(i + 1) * 128, :],
                          writes=[B(f"wd_{nm}")], slot=f"wd_{nm}")

            for i in range(NCH):
                P.dma(QS, s_h16[1][:, 2048:3072],
                      Wo_d[i * 128:(i + 1) * 128, :],
                      reads=[B("wd_wo")], writes=[bH3s])
                with P.task(VE) as t:
                    t.op(lambda e, i=i: e.tensor_copy(s_lnb16[:],
                                                      vcol("lnb", i)),
                         reads=[ST], writes=[B("lnb16")])
                with P.task(TE) as t:
                    t.op(lambda e, i=i: e.matmul(
                        pO[0:1, 0:512], s_lnb16[:],
                        s_h16[1][:, 2048:2560], start=(i == 0),
                        stop=(i == NCH - 1)),
                        reads=[bH3s, B("lnb16"), ST],
                        writes=[bpO[0], bpO[1]], waw=(i == 0))
                    t.op(lambda e, i=i: e.matmul(
                        pO[0:1, 512:1024], s_lnb16[:],
                        s_h16[1][:, 2560:3072], start=(i == 0),
                        stop=(i == NCH - 1)),
                        reads=[bH3s, B("lnb16"), ST],
                        writes=[bpO[2], bpO[3]], waw=(i == 0))
            with P.task(VE) as t:
                t.op(lambda e: e.tensor_copy(s_lnbWo[:], pO[0:1, :]),
                     reads=[bpO[0], bpO[1], bpO[2], bpO[3]], writes=[ST])
            with P.task(VE) as t:
                t.op(lambda e: e.memset(s_state[:], 0.0),
                     writes=[B(f"mst{i}") for i in range(128)])
            with P.task(VE) as t:
                t.op(lambda e: e.memset(s_S[:], 0.0), writes=[B("wS")])
            with P.task(VE) as t:
                t.op(lambda e: e.memset(s_xlnf[:], 0.0), writes=[B("xlnf")])
            with P.task(VE) as t:
                t.op(lambda e: e.memset(s_x3f[:], 0.0), writes=[B("x3f")])
            with P.task(VE) as t:
                t.op(lambda e: e.memset(s_h16[1][:, 0:1], 0.0),
                     reads=[bH3t, bH3s], writes=[bH3e])

        def ln_apply(tile, dst, dst_buf):
            xr = s_xres[xslot(tile)]
            mb = B("t_bn")
            with P.task(VE) as t:
                t.op(lambda e, xr=xr: e.bn_stats(s_bn[:, 0:6], xr[:, 0:512]),
                     reads=[xrb(tile)], writes=[mb])
                t.op(lambda e, xr=xr: e.bn_stats(s_bn[:, 6:12],
                                                 xr[:, 512:1024]),
                     reads=[xrb(tile)], writes=[mb])
                t.op(lambda e: e.bn_aggr(s_bn[:, 12:14], s_bn[:, 0:12]),
                     reads=[mb], writes=[mb])
            rs = stcol("ln_rs")
            rsb = B("ln_rsb")
            with P.task(SC) as t:
                t.op(lambda e, rs=rs: e.activation(rs, s_bn[:, 13:14], AF.Ln,
                                                   bias=LN_EPS),
                     reads=[mb], writes=[rsb])
                t.op(lambda e, rs=rs: e.activation(rs, rs, AF.Exp, scale=-0.5),
                     reads=[rsb], writes=[rsb])
            with P.task(VE) as t:
                t.op(lambda e, rs=rs, xr=xr, dst=dst: e.tensor_scalar(
                    dst, xr[:, :], s_bn[:, 12:13], rs, ALU.subtract, ALU.mult),
                    reads=[xrb(tile), mb, rsb], writes=[dst_buf])

        def stage_A(w):
            t0 = w * NTT
            if w > 0:
                with P.task(GP) as t:
                    for cch in range(NCH):
                        t.op(lambda e, cch=cch: e.tensor_copy(
                            xlnf(cch)[:, 0:1], xlnf(cch)[:, TCM:TCM + 1]),
                            reads=[B("xlnf")], writes=[B("xlnf")])
            ntl = NTT + (1 if w + 1 < NW else 0)
            for k in range(ntl):
                tile = t0 + k
                xr = s_xres[xslot(tile)]
                P.dma(QS, xr[:, :], x_in[tile * 128:(tile + 1) * 128, :],
                      writes=[xrb(tile)])
                xt = s_xtok[tile % 2]
                xtb = B(f"xtok{tile % 2}")
                if A_XLN < 1:
                    continue
                ln_apply(tile, xt[:, :], xtb)
                if A_XLN < 2:
                    continue
                for cch in range(NCH):
                    psl = cch % 2
                    ptile = pMb if psl == 0 else pNb
                    ptb = bM if psl == 0 else bN
                    with P.task(TE) as t:
                        t.op(lambda e, cch=cch, ptile=ptile, xt=xt: e.transpose(
                            ptile[:, 0:128],
                            xt[:, cch * 128:(cch + 1) * 128], s_ident[:]),
                            reads=[xtb, ST], writes=[ptb])
                    with P.task(SC) as t:
                        if k < NTT:
                            dst = xlnf(cch)[:, 1 + k * 128:1 + (k + 1) * 128]
                            src = ptile[:, 0:128]
                        else:
                            dst = xlnf(cch)[:, TCM + 1:TCM + 2]
                            src = ptile[:, 0:1]
                        t.op(lambda e, dst=dst, src=src: e.copy(dst, src),
                             reads=[ptb], writes=[B("xlnf")])
            if w + 1 >= NW:
                with P.task(VE) as t:
                    for cch in range(NCH):
                        t.op(lambda e, cch=cch: e.memset(
                            xlnf(cch)[:, TCM + 1:TCM + 2], 0.0),
                            writes=[B("xlnf")])

            if not A_BCDT:
                return
            for cch in range(NCH):
                cvb = B("a_cvt")
                ub = cvb
                xf = xlnf(cch)
                with P.task(VE) as t:
                    t.op(lambda e, cch=cch, xf=xf: e.tensor_scalar_mul(
                        s_cvt[:, :], xf[:, 1:TCM + 1],
                        s_convk[:, 3 * cch + 1:3 * cch + 2]),
                        reads=[B("xlnf"), ST], writes=[cvb])
                    t.op(lambda e, cch=cch, xf=xf: e.scalar_tensor_tensor(
                        s_cvt[:, :], xf[:, 0:TCM],
                        s_convk[:, 3 * cch:3 * cch + 1],
                        s_cvt[:, :], ALU.mult, ALU.add),
                        reads=[B("xlnf"), ST, cvb], writes=[cvb])
                    t.op(lambda e, cch=cch, xf=xf: e.scalar_tensor_tensor(
                        s_cvt[:, :], xf[:, 2:TCM + 2],
                        s_convk[:, 3 * cch + 2:3 * cch + 3],
                        s_cvt[:, :], ALU.mult, ALU.add),
                        reads=[B("xlnf"), ST, cvb], writes=[cvb])
                with P.task(SC) as t:
                    t.op(lambda e, cch=cch: e.activation(
                        s_u[:, :], s_cvt[:, :], AF.Silu),
                        reads=[cvb], writes=[ub])
                with P.task(TE) as t:
                    t.op(lambda e, cch=cch: e.matmul(
                        pB_[0:32, 0:TCM], s_Wbc[:, cch * 32:(cch + 1) * 32],
                        s_u[:, :], start=(cch == 0),
                        stop=(cch == NCH - 1)),
                        reads=[ub, ST], writes=pB_ALL, waw=(cch == 0))
                    t.op(lambda e, cch=cch: e.matmul(
                        pM[0:64, 0:TCM], s_Wdt[:, cch * 64:(cch + 1) * 64],
                        s_u[:, :], start=(cch == 0),
                        stop=(cch == NCH - 1)),
                        reads=[ub, ST], writes=pM_ALL, waw=(cch == 0))
            with P.task(VE) as t:
                t.op(lambda e: e.tensor_scalar_add(s_bc16[:, :],
                                                   pB_[0:32, 0:TCM], s_bbc[:]),
                     reads=pB_ALL + [ST], writes=[B("bc16")])
            for g in range(8):
                for j in range(4):
                    r = g * 4 + j
                    sl2 = r % 2
                    P.dma(QA, s_bcr[0:1, sl2 * TCM:(sl2 + 1) * TCM],
                          s_bc16[r:r + 1, :],
                          reads=[B("bc16")], writes=[B(f"bcr{sl2}")],
                          slot=f"bcr{sl2}")
                    if r < NSSM:
                        dstb = B(f"Brep{r}")
                        dst = s_Brep[:, r * TCM:(r + 1) * TCM]
                    else:
                        dstb = B(f"Crep{r - NSSM}")
                        dst = s_Crep[:, (r - NSSM) * TCM:
                                     (r - NSSM + 1) * TCM]
                    psb, pst = (bB, pB_) if r % 2 == 0 else (bY, pY)
                    with P.task(TE) as t:
                        t.op(lambda e, sl2=sl2, pst=pst: e.matmul(
                            pst[:, 0:TCM], s_ones[0:1, :],
                            s_bcr[0:1, sl2 * TCM:(sl2 + 1) * TCM],
                            start=True, stop=True),
                            reads=[B(f"bcr{sl2}"), ST], writes=[psb])
                    with P.task(SC) as t:
                        t.op(lambda e, dst=dst, pst=pst: e.copy(
                            dst, pst[:, 0:TCM]),
                            reads=[psb], writes=[dstb])
            with P.task(VE) as t:
                t.op(lambda e: e.tensor_scalar_add(s_dtpj[:, :],
                                                   pM[0:64, 0:TCM], s_bdt[:]),
                     reads=pM_ALL + [ST], writes=[B("dtp")])

            for cch in range(NCH):
                dtb = B("dt0")
                with P.task(TE) as t:
                    t.op(lambda e, cch=cch: e.matmul(
                        pM[:, 0:TCM], s_Wdtp[:, cch * 128:(cch + 1) * 128],
                        s_dtpj[:, :], start=True, stop=True),
                        reads=[B("dtp"), ST], writes=pM_ALL)
                with P.task(SC) as t:
                    t.op(lambda e, cch=cch: e.activation(
                        s_e1[:, :], pM[:, 0:TCM], AF.Exp,
                        bias=vcol("bdtp", cch)),
                        reads=pM_ALL + [ST], writes=[B("t_e1")])
                    t.op(lambda e, cch=cch: e.activation(
                        s_dt[:, :], s_e1[:, :], AF.Ln, bias=1.0),
                        reads=[B("t_e1")], writes=[dtb])
                xdtb = B(f"xdt{cch % 2}")
                with P.task(VE) as t:
                    t.op(lambda e, cch=cch: e.tensor_mul(
                        s_xdt[cch % 2][:, :], xlnf(cch)[:, 1:TCM + 1],
                        s_dt[:, :]),
                        reads=[B("xlnf"), dtb], writes=[xdtb])
                nlim = min(NSSM, A_NLIM)

                def _beta(t, n):
                    par = n % 2
                    t.op(lambda e, par=par, cch=cch, n=n: e.tensor_mul(
                        s_be[par][:, :],
                        s_Brep[:, n * TCM:(n + 1) * TCM],
                        s_xdt[cch % 2][:, :]),
                        reads=[B(f"Brep{n}"), xdtb], writes=[B(f"be{par}")])

                def _chh(t, n):
                    par = n % 2
                    t.op(lambda e, par=par, n=n: e.tensor_mul(
                        s_chh[par][:, :],
                        s_Crep[:, n * TCM:(n + 1) * TCM],
                        s_hh[par][:, :]),
                        reads=[B(f"Crep{n}"), B(f"h{par}")],
                        writes=[B(f"ch{par}")])

                def _accum(n):
                    par = n % 2
                    with P.task(TE) as t:
                        t.op(lambda e, par=par, n=n: e.matmul(
                            pY[:, 0:TCM], s_ident[:], s_chh[par][:, :],
                            start=(n == 0), stop=(n == NSSM - 1)),
                            reads=[B(f"ch{par}"), ST], writes=[bpY],
                            waw=(n == 0))

                if nlim:
                    with P.task(VE) as t:
                        _beta(t, 0)
                for n in range(nlim):
                    par = n % 2
                    alb = B(f"al{par}")
                    hb = B(f"h{par}")
                    stc = cch * NSSM + n
                    with P.task(SC) as t:
                        t.op(lambda e, cch=cch, n=n, par=par: e.activation(
                            s_al[par][:, :], s_dt[:, :], AF.Exp,
                            scale=s_Acoef[:, cch * NSSM + n:
                                          cch * NSSM + n + 1]),
                            reads=[dtb, ST], writes=[alb])
                    if False:
                        with P.task(GP) as t:
                            t.op(lambda e, par=par, stc=stc:
                                 e.tensor_tensor_scan(
                                     s_hh[par][:, :], s_al[par][:, :],
                                     s_be[par][:, :],
                                     s_state[:, stc:stc + 1],
                                     ALU.mult, ALU.add),
                                 reads=[alb, B(f"be{par}"), B(f"mst{stc}")],
                                 writes=[hb])
                        with P.task(VE) as t:
                            if n + 1 < nlim:
                                _beta(t, n + 1)
                            if n > 0:
                                _chh(t, n - 1)
                    else:
                        with P.task(VE) as t:
                            if A_SCAN:
                                t.op(lambda e, par=par, stc=stc:
                                     e.tensor_tensor_scan(
                                         s_hh[par][:, :], s_al[par][:, :],
                                         s_be[par][:, :],
                                         s_state[:, stc:stc + 1],
                                         ALU.mult, ALU.add),
                                     reads=[alb, B(f"be{par}"),
                                            B(f"mst{stc}")],
                                     writes=[hb])
                            else:
                                t.op(lambda e, par=par: e.tensor_copy(
                                    s_hh[par][:, :], s_be[par][:, :]),
                                    reads=[alb, B(f"be{par}"),
                                           B(f"mst{stc}")],
                                    writes=[hb])
                            if n + 1 < nlim:
                                _beta(t, n + 1)
                            if n > 0:
                                _chh(t, n - 1)
                    with P.task(GP) as t:
                        t.op(lambda e, par=par, stc=stc: e.tensor_copy(
                            s_state[:, stc:stc + 1],
                            s_hh[par][:, TCM - 1:TCM]),
                            reads=[hb], writes=[B(f"mst{stc}")])
                    if n > 0:
                        _accum(n - 1)
                if nlim:
                    with P.task(VE) as t:
                        _chh(t, nlim - 1)
                    _accum(nlim - 1)
                if not A_TAIL:
                    continue
                yfb = B("yf")
                with P.task(VE) as t:
                    t.op(lambda e, cch=cch: e.scalar_tensor_tensor(
                        s_yf[:, :], xlnf(cch)[:, 1:TCM + 1], vcol("D", cch),
                        pY[:, 0:TCM], ALU.mult, ALU.add),
                        reads=[bpY, B("xlnf"), ST], writes=[yfb])
                for k in range(NTT):
                    tile = t0 + k
                    psl = k % 2
                    ptile = pMb if psl == 0 else pNb
                    ptb = bM if psl == 0 else bN
                    with P.task(TE) as t:
                        t.op(lambda e, k=k, ptile=ptile: e.transpose(
                            ptile[:, 0:128],
                            s_yf[:, k * 128:(k + 1) * 128], s_ident[:]),
                            reads=[yfb, ST], writes=[ptb])
                    with P.task(VE) as t:
                        t.op(lambda e, cch=cch, tile=tile, ptile=ptile:
                             e.tensor_add(
                                 s_xres[xslot(tile)][:, cch * 128:
                                                     (cch + 1) * 128],
                                 s_xres[xslot(tile)][:, cch * 128:
                                                     (cch + 1) * 128],
                                 ptile[:, 0:128]),
                             reads=[ptb, xrb(tile)], writes=[xrb(tile)])

        RND_PROD = (e1_i32, zq_i32)
        RND_PRODB = (B("t_e1"), bZQ)
        RND_DIFF = (pY, pB_)
        RND_DIFFB = (bY, bB)

        def rnd_prod(s, src, src_bufs, scale, scale_bufs):
            with P.task(VE) as t:
                t.op(lambda e, s=s, src=src, scale=scale: e.tensor_scalar(
                    RND_PROD[s][:, 0:512], src, scale, None, ALU.mult),
                    reads=list(src_bufs) + list(scale_bufs),
                    writes=[RND_PRODB[s]])

        def rnd_cast(s):
            with P.task(SC) as t:
                t.op(lambda e, s=s: e.copy(
                    rn2_bf[:, s * 512:(s + 1) * 512], RND_PROD[s][:, 0:512]),
                    reads=[RND_PRODB[s]], writes=[B(f"rnb{s}")])

        def rnd_tail(s, src, src_bufs, scale, scale_bufs, dst, dstb,
                     clip=False):
            rn = rn2_bf[:, s * 512:(s + 1) * 512]
            with P.task(VE) as t:
                t.op(lambda e, s=s, src=src, scale=scale, rn=rn:
                     e.scalar_tensor_tensor(
                         RND_DIFF[s][:, 0:512], src, scale, rn,
                         ALU.mult, ALU.subtract),
                     reads=list(src_bufs) + list(scale_bufs) +
                     [B(f"rnb{s}")], writes=[RND_DIFFB[s]])
                t.op(lambda e, s=s, rn=rn: e.scalar_tensor_tensor(
                    rn, RND_DIFF[s][:, 0:512], 0.5, rn,
                    ALU.is_gt, ALU.add),
                    reads=[RND_DIFFB[s], B(f"rnb{s}")],
                    writes=[B(f"rnb{s}")])
                t.op(lambda e, s=s, rn=rn, dst=dst: e.scalar_tensor_tensor(
                    dst, RND_DIFF[s][:, 0:512], -0.5, rn,
                    ALU.is_lt, ALU.subtract),
                    reads=[RND_DIFFB[s], B(f"rnb{s}")], writes=[dstb])
                if clip:
                    t.op(lambda e, dst=dst: e.tensor_scalar(
                        dst, dst, -1.0, 1.0, ALU.max, ALU.min),
                        reads=[dstb], writes=[dstb])

        def stage_Bq(w, pair):
            t0 = w * NTT
            for k in (2 * pair, 2 * pair + 1):
                tile = t0 + k
                xt = s_xtok[tile % 2]
                xtb = B(f"xtok{tile % 2}")
                ln_apply(tile, xt[:, :], xtb)
                sq = stcol(f"q_sq{k}")
                mx = stcol(f"q_mx{k}")
                svb = B(f"q_sv{k}")
                with P.task(SC) as t:
                    t.op(lambda e, sq=sq, xt=xt: e.activation(
                        s_zq[:, 0:512], xt[:, 0:512], AF.Square,
                        accum_out=sq),
                        reads=[xtb], writes=[svb, bZQ])
                    t.op(lambda e, xt=xt, k=k: e.activation(
                        s_zq[:, 0:512], xt[:, 512:1024], AF.Square,
                        accum_out=stcol(f"q_sqb{k}")),
                        reads=[xtb], writes=[svb, bZQ])
                with P.task(VE) as t:
                    t.op(lambda e, sq=sq, k=k: e.tensor_add(
                        sq, sq, stcol(f"q_sqb{k}")),
                        reads=[svb], writes=[svb])
                    t.op(lambda e, mx=mx, xt=xt: e.tensor_reduce(
                        mx, xt[:, :], AX.X, ALU.max,
                        apply_absolute_value=True),
                        reads=[xtb], writes=[svb])
                rs = stcol(f"q_rs{k}")
                mc = stcol(f"q_mc{k}")
                rq = stcol(f"q_rq{k}")
                iv = stcol(f"q_iv{k}")
                sts = stcol(f"q_st{k}")
                with P.task(SC) as t:
                    t.op(lambda e, rs=rs, sq=sq: e.activation(
                        rs, sq, AF.Ln, scale=1.0 / C, bias=RMS_EPS),
                        reads=[svb], writes=[svb])
                    t.op(lambda e, rs=rs: e.activation(rs, rs, AF.Exp,
                                                       scale=-0.5),
                         reads=[svb], writes=[svb])
                with P.task(VE) as t:
                    t.op(lambda e, mc=mc, mx=mx, rs=rs: e.tensor_scalar(
                        mc, mx, rs, QEPS, ALU.mult, ALU.max),
                        reads=[svb], writes=[svb])
                    t.op(lambda e, rq=rq, mc=mc: e.reciprocal(rq, mc),
                         reads=[svb], writes=[svb])
                    t.op(lambda e, rq=rq, rs=rs: e.tensor_scalar(
                        rq, rq, rs, 127.0, ALU.mult, ALU.mult),
                        reads=[svb], writes=[svb])
                    t.op(lambda e, iv=iv, mc=mc: e.tensor_scalar(
                        iv, mc, s_wqs[:, 2:3], 1.0 / 127.0,
                        ALU.mult, ALU.mult),
                        reads=[svb, ST], writes=[svb])
                    t.op(lambda e, sts=sts, iv=iv: e.reciprocal(sts, iv),
                         reads=[svb], writes=[svb])
                stsrb = B(f"scrow{k}")
                with P.task(TE) as t:
                    t.op(lambda e, sts=sts: e.transpose(
                        pB_[0:1, 384:512], sts, s_id32[:]),
                        reads=[svb, ST], writes=[bpB[3]])
                with P.task(VE) as t:
                    t.op(lambda e, k=k: e.tensor_copy(
                        s_scb[0:1, k * 128:(k + 1) * 128], pB_[0:1, 384:512]),
                        reads=[bpB[3]], writes=[stsrb])
                xqb = B(f"xq1f{k % 2}")
                QDST = (s_cvt[:, 0:512], s_yf[:, 0:512])
                QDSTB = (B("a_cvt"), B("yf"))
                for half in range(2):
                    rnd_prod(half, xt[:, half * 512:(half + 1) * 512],
                             [xtb, svb], rq, [])
                for half in range(2):
                    rnd_cast(half)
                for half in range(2):
                    rnd_tail(half, xt[:, half * 512:(half + 1) * 512],
                             [xtb, svb], rq, [], QDST[half], QDSTB[half])
                    for j in range(4):
                        cch = half * 4 + j
                        ptile = pMb if j % 2 == 0 else pNb
                        ptb = bM if j % 2 == 0 else bN
                        with P.task(TE) as t:
                            t.op(lambda e, j=j, ptile=ptile, half=half:
                                 e.transpose(
                                     ptile[:, 0:128],
                                     QDST[half][:, j * 128:(j + 1) * 128],
                                     s_ident[:]),
                                 reads=[QDSTB[half], ST], writes=[ptb])
                        with P.task(SC if j % 2 == 0 else VE) as t:
                            fn = (lambda e, cch=cch, ptile=ptile, k=k:
                                  e.copy(
                                      s_xq1f[k % 2][:, cch * 128:
                                                    (cch + 1) * 128],
                                      ptile[:, 0:128])) if j % 2 == 0 else (
                                lambda e, cch=cch, ptile=ptile, k=k:
                                e.tensor_copy(
                                    s_xq1f[k % 2][:, cch * 128:
                                                  (cch + 1) * 128],
                                    ptile[:, 0:128]))
                            t.op(fn, reads=[ptb], writes=[xqb])

        def stage_F(w, pair):
            t0 = w * NTT
            ks = (2 * pair, 2 * pair + 1)
            for hs in range(HF // 512):
                P.dma(QS, s_b1s[hs % 2][:, :],
                      b1_d[0:1, hs * 512:(hs + 1) * 512],
                      reads=[B("b1d")], writes=[B(f"b1s{hs % 2}")])
                for cch in range(NCH):
                    slot = (hs * NCH + cch) % NW1R
                    P.dma(QS, s_w1r[slot][:, :],
                          w1T_d[cch * 128:(cch + 1) * 128,
                                hs * 512:(hs + 1) * 512],
                          writes=[B(f"w1r{slot}")])
                for k in ks:
                    pp = pF[:, (hs % 2) * 512:(hs % 2) * 512 + 512]
                    ppb = [bpF[2 * (hs % 2)], bpF[2 * (hs % 2) + 1]]
                    iv = stcol(f"q_iv{k}")
                    with P.task(TE) as t:
                        t.op(lambda e, pp=pp, k=k, hs=hs: e.matmul(
                            pp, s_scb[0:1, k * 128:(k + 1) * 128],
                            s_b1s[hs % 2][:, :],
                            start=True, stop=False),
                            reads=[B(f"scrow{k}"), B(f"b1s{hs % 2}"), ST],
                            writes=ppb)
                        for cch in range(NCH):
                            slot = (hs * NCH + cch) % NW1R
                            t.op(lambda e, pp=pp, cch=cch, k=k, slot=slot:
                                 e.matmul(
                                     pp,
                                     s_xq1f[k % 2][:, cch * 128:
                                                   (cch + 1) * 128],
                                     s_w1r[slot][:, :],
                                     start=False, stop=(cch == NCH - 1)),
                                 reads=[B(f"xq1f{k % 2}"), B(f"w1r{slot}")],
                                 writes=ppb, waw=False)
                    with P.task(SC) as t:
                        t.op(lambda e, pp=pp, k=k, hs=hs, iv=iv: e.activation(
                            s_h16[k % 2][:, hs * 512:(hs + 1) * 512], pp,
                            AF.Gelu, scale=iv),
                            reads=ppb + [B(f"q_sv{k}")],
                            writes=[B(f"h16{k % 2}")])
            for k in ks:
                h16 = s_h16[k % 2]
                h16b = B(f"h16{k % 2}")
                sq2 = stcol(f"f_sq{k}")
                mx2 = stcol(f"f_mx{k}")
                s2b = B(f"f_s{k}")
                pj = [stcol(f"f_p{j}") for j in range(8)]
                with P.task(SC) as t:
                    for j in range(8):
                        t.op(lambda e, j=j, h16=h16: e.activation(
                            s_zq[:, 0:512], h16[:, j * 512:(j + 1) * 512],
                            AF.Square, accum_out=pj[j]),
                            reads=[h16b], writes=[s2b, bZQ])
                with P.task(VE) as t:
                    t.op(lambda e, sq2=sq2: e.tensor_reduce(
                        sq2, s_stat[:, _statmap["f_p0"]:
                                    _statmap["f_p0"] + 8],
                        AX.X, ALU.add),
                        reads=[s2b], writes=[s2b])
                    t.op(lambda e, mx2=mx2, h16=h16: e.tensor_reduce(
                        mx2, h16[:, :], AX.X, ALU.max,
                        apply_absolute_value=True),
                        reads=[h16b], writes=[s2b])
                rs2 = stcol(f"f_rs{k}")
                mc2 = stcol(f"f_mc{k}")
                rq2 = stcol(f"f_rq{k}")
                iv2 = stcol(f"f_iv{k}")
                st2 = stcol(f"f_st{k}")
                with P.task(SC) as t:
                    t.op(lambda e, rs2=rs2, sq2=sq2: e.activation(
                        rs2, sq2, AF.Ln, scale=1.0 / HF, bias=RMS_EPS),
                        reads=[s2b], writes=[s2b])
                    t.op(lambda e, rs2=rs2: e.activation(rs2, rs2, AF.Exp,
                                                         scale=-0.5),
                         reads=[s2b], writes=[s2b])
                with P.task(VE) as t:
                    t.op(lambda e, mc2=mc2, mx2=mx2, rs2=rs2: e.tensor_scalar(
                        mc2, mx2, rs2, QEPS, ALU.mult, ALU.max),
                        reads=[s2b], writes=[s2b])
                    t.op(lambda e, rq2=rq2, mc2=mc2: e.reciprocal(rq2, mc2),
                         reads=[s2b], writes=[s2b])
                    t.op(lambda e, rq2=rq2, rs2=rs2: e.tensor_scalar(
                        rq2, rq2, rs2, 127.0, ALU.mult, ALU.mult),
                        reads=[s2b], writes=[s2b])
                    t.op(lambda e, iv2=iv2, mc2=mc2: e.tensor_scalar(
                        iv2, mc2, s_wqs[:, 3:4], 1.0 / 127.0,
                        ALU.mult, ALU.mult),
                        reads=[s2b, ST], writes=[s2b])
                    t.op(lambda e, st2=st2, iv2=iv2: e.reciprocal(st2, iv2),
                         reads=[s2b], writes=[s2b])
                st2rb = B(f"scrow{k}")
                with P.task(TE) as t:
                    t.op(lambda e, st2=st2: e.transpose(
                        pB_[0:1, 384:512], st2, s_id32[:]),
                        reads=[s2b, ST], writes=[bpB[3]])
                with P.task(VE) as t:
                    t.op(lambda e, k=k: e.tensor_copy(
                        s_scb[0:1, k * 128:(k + 1) * 128],
                        pB_[0:1, 384:512]),
                        reads=[bpB[3]], writes=[st2rb])
            if True:
                P.dma(QS, s_b1s[0][:, :], b2_d[0:1, 0:512],
                      reads=[B("b2d")], writes=[B("b1s0")])
                P.dma(QS, s_b1s[1][:, :], b2_d[0:1, 512:1024],
                      reads=[B("b2d")], writes=[B("b1s1")])
                for kk, k in enumerate(ks):
                    ppo = pO if kk == 0 else pF
                    ppob = ALL_pO if kk == 0 else ALL_pF
                    with P.task(TE) as t:
                        t.op(lambda e, ppo=ppo, k=k: e.matmul(
                            ppo[:, 0:512],
                            s_scb[0:1, k * 128:(k + 1) * 128],
                            s_b1s[0][:, :], start=True, stop=False),
                            reads=[B(f"scrow{k}"), B("b1s0"), ST],
                            writes=ppob)
                        t.op(lambda e, ppo=ppo, k=k: e.matmul(
                            ppo[:, 512:1024],
                            s_scb[0:1, k * 128:(k + 1) * 128],
                            s_b1s[1][:, :], start=True, stop=False),
                            reads=[B(f"scrow{k}"), B("b1s1"), ST],
                            writes=ppob, waw=False)
                FDST = (s_cvt[:, 0:512], s_yf[:, 0:512])
                FDSTB = (B("a_cvt"), B("yf"))
                for qq in range(9):
                    if qq < 8:
                        for kk, k in enumerate(ks):
                            rnd_prod(kk,
                                     s_h16[k % 2][:, qq * 512:(qq + 1) * 512],
                                     [B(f"h16{k % 2}"), B(f"f_s{k}")],
                                     stcol(f"f_rq{k}"), [])
                    if qq > 0:
                        qp = qq - 1
                        for kk, k in enumerate(ks):
                            rnd_tail(kk,
                                     s_h16[k % 2][:, qp * 512:(qp + 1) * 512],
                                     [B(f"h16{k % 2}"), B(f"f_s{k}")],
                                     stcol(f"f_rq{k}"), [],
                                     FDST[kk], FDSTB[kk])
                            for j in range(4):
                                hc = qp * 4 + j
                                ptile = pMb if j % 2 == 0 else pNb
                                ptb = bM if j % 2 == 0 else bN
                                with P.task(TE) as t:
                                    t.op(lambda e, j=j, ptile=ptile, kk=kk:
                                         e.transpose(
                                             ptile[:, 0:128],
                                             FDST[kk][:, j * 128:
                                                      (j + 1) * 128],
                                             s_ident[:]),
                                         reads=[FDSTB[kk], ST], writes=[ptb])
                                with P.task(SC if j % 2 == 0 else VE) as t:
                                    fn = (lambda e, hc=hc, kk=kk, ptile=ptile:
                                          e.copy(
                                              xq2t_slot(hc, kk),
                                              ptile[:, 0:128])
                                          ) if j % 2 == 0 else (
                                        lambda e, hc=hc, kk=kk, ptile=ptile:
                                        e.tensor_copy(
                                            xq2t_slot(hc, kk),
                                            ptile[:, 0:128]))
                                    t.op(fn, reads=[ptb],
                                         writes=[B(f"xq2t{hc % 4}")])
                    if qq < 8:
                        for kk, k in enumerate(ks):
                            rnd_cast(kk)
                    if qq > 0:
                        qp = qq - 1
                        for j in range(4):
                            hc = qp * 4 + j
                            for ch in range(2):
                                wsl = (hc * 2 + ch) % 4
                                wb = B(f"w2s{wsl}")
                                P.dma(QS, s_w2s[wsl][:, :],
                                      w2T_d[hc * 128:(hc + 1) * 128,
                                            ch * 512:(ch + 1) * 512],
                                      writes=[wb])
                                for kk, k in enumerate(ks):
                                    ppo = pO if kk == 0 else pF
                                    ppob = ALL_pO if kk == 0 else ALL_pF
                                    with P.task(TE) as t:
                                        t.op(lambda e, ppo=ppo, hc=hc, kk=kk,
                                             wsl=wsl, ch=ch: e.matmul(
                                                 ppo[:, ch * 512:
                                                     (ch + 1) * 512],
                                                 xq2t_slot(hc, kk),
                                                 s_w2s[wsl][:, :],
                                                 start=False,
                                                 stop=(hc == 31)),
                                             reads=[B(f"xq2t{hc % 4}"), wb],
                                             writes=ppob, waw=False)
                for kk, k in enumerate(ks):
                    tile = t0 + k
                    ppo = pO if kk == 0 else pF
                    ppob = ALL_pO if kk == 0 else ALL_pF
                    iv2 = stcol(f"f_iv{k}")
                    with P.task(VE) as t:
                        t.op(lambda e, ppo=ppo, tile=tile, iv2=iv2:
                             e.scalar_tensor_tensor(
                                 s_xres[xslot(tile)][:, :], ppo[:, :], iv2,
                                 s_xres[xslot(tile)][:, :],
                                 ALU.mult, ALU.add),
                             reads=ppob + [B(f"f_s{k}"), xrb(tile)],
                             writes=[xrb(tile)])

        def wkv_decay(mixb):
            mix = s_mix
            for cch in range(NCH):
                with P.task(TE) as t:
                    t.op(lambda e, cch=cch: e.matmul(
                        pF[0:64, 768:768 + TCW],
                        s_dw1[:, cch * 64:(cch + 1) * 64],
                        mix[:, cch * TCW:(cch + 1) * TCW],
                        start=(cch == 0), stop=(cch == NCH - 1)),
                        reads=[mixb, ST], writes=[bpF[3]], waw=(cch == 0))
            with P.task(SC) as t:
                t.op(lambda e: e.activation(s_dec[:, :],
                                            pF[0:64, 768:768 + TCW], AF.Tanh),
                     reads=[bpF[3]], writes=[B("dtp")])
            for cch in range(NCH):
                with P.task(TE) as t:
                    t.op(lambda e, cch=cch: e.matmul(
                        pF[:, 768:768 + TCW],
                        s_dw2[:, cch * 128:(cch + 1) * 128], s_dec[:, :],
                        start=True, stop=True),
                        reads=[B("dtp"), ST], writes=[bpF[3]])
                with P.task(SC) as t:
                    t.op(lambda e, cch=cch: e.activation(
                        s_e1[:, 0:TCW], pF[:, 768:768 + TCW], AF.Exp,
                        bias=vcol("td", cch)),
                        reads=[bpF[3], ST], writes=[B("t_e1")])
                    t.op(lambda e, cch=cch: e.activation(
                        s_emw[:, cch * TCW:(cch + 1) * TCW],
                        s_e1[:, 0:TCW], AF.Exp, scale=-1.0),
                        reads=[B("t_e1")], writes=[B("emw")])
                with P.task(VE) as t:
                    for q in range(NQ):
                        t.op(lambda e, q=q: e.tensor_tensor_scan(
                            s_EE[:, q * 128:(q + 1) * 128], s_ones1[:, :],
                            s_e1[:, q * 128:(q + 1) * 128], 0.0,
                            ALU.mult, ALU.add),
                            reads=[B("t_e1"), ST], writes=[B("EE")])
                with P.task(SC) as t:
                    t.op(lambda e, cch=cch: e.activation(
                        s_frw[:, cch * TCW + 1:(cch + 1) * TCW],
                        s_EE[:, 0:TCW - 1], AF.Exp, scale=-1.0),
                        reads=[B("EE")], writes=[B("frw")])
                    t.op(lambda e, cch=cch: e.activation(
                        s_fkw[:, cch * TCW:(cch + 1) * TCW], s_EE[:, 0:TCW],
                        AF.Exp),
                        reads=[B("EE")], writes=[B("fkw")])
                with P.task(VE) as t:
                    t.op(lambda e, cch=cch: e.memset(
                        s_frw[:, cch * TCW:(cch + 1) * TCW]
                        .rearrange("p (q a) -> p q a", a=128)[:, :, 0:1], 1.0),
                        writes=[B("frw")])
                with P.task(SC) as t:
                    pass
                    for q in range(NQ):
                        t.op(lambda e, cch=cch, q=q: e.activation(
                            s_eQw[:, cch * NQ + q:cch * NQ + q + 1],
                            s_EE[:, (q + 1) * 128 - 1:(q + 1) * 128],
                            AF.Exp, scale=-1.0),
                            reads=[B("EE")], writes=[B("eQw")])

        def wkv_proj(mixb, dstf, dstb, fsrc, fb):
            mix = s_mix
            Wd = Wr_d if dstf is s_rt else Wk_d
            for half in range(2):
                for cin in range(NCH):
                    wsl = cin % 2
                    wb = B(f"wrk{wsl}")
                    P.dma(QS, s_wrk[wsl][:, :],
                          Wd[cin * 128:(cin + 1) * 128,
                             half * 512:(half + 1) * 512],
                          reads=[B("wd_wr") if Wd is Wr_d else B("wd_wk")],
                          writes=[wb])
                    with P.task(TE) as t:
                        for co4 in range(4):
                            pp = pF if co4 < 2 else pO
                            off = (co4 % 2) * 512
                            ppb = ([bpF[0], bpF[1]] if co4 == 0 else
                                   [bpF[2], bpF[3]] if co4 == 1 else
                                   [bpO[0], bpO[1]] if co4 == 2 else
                                   [bpO[2], bpO[3]])
                            t.op(lambda e, pp=pp, off=off, cin=cin, co4=co4,
                                 wsl=wsl: e.matmul(
                                     pp[:, off:off + TCW],
                                     s_wrk[wsl][:, co4 * 128:(co4 + 1) * 128],
                                     mix[:, cin * TCW:(cin + 1) * TCW],
                                     start=(cin == 0),
                                     stop=(cin == NCH - 1)),
                                 reads=[mixb, wb], writes=ppb,
                                 waw=(cin == 0))
                for co4 in range(4):
                    cout = half * 4 + co4
                    pp = pF if co4 < 2 else pO
                    off = (co4 % 2) * 512
                    ppb = ([bpF[0], bpF[1]] if co4 == 0 else
                           [bpF[2], bpF[3]] if co4 == 1 else
                           [bpO[0], bpO[1]] if co4 == 2 else
                           [bpO[2], bpO[3]])
                    with P.task(VE) as t:
                        t.op(lambda e, pp=pp, off=off, cout=cout, dstf=dstf,
                             fsrc=fsrc: e.tensor_mul(
                                 dstf[:, cout * TCW:(cout + 1) * TCW],
                                 pp[:, off:off + TCW],
                                 fsrc[:, cout * TCW:(cout + 1) * TCW]),
                             reads=ppb + [fb], writes=[dstb])

        def wkv_vproj(mixb):
            mix = s_mix
            for cin in range(NCH):
                wsl = (2 * cin) % 4
                wb0, wb1 = B(f"wvo{wsl}"), B(f"wvo{wsl + 1}")
                P.dma(QS, s_wvo[wsl][:, :],
                      Wv_d[cin * 128:(cin + 1) * 128, 0:512],
                      reads=[B("wd_wv")], writes=[wb0])
                P.dma(QS, s_wvo[wsl + 1][:, :],
                      Wv_d[cin * 128:(cin + 1) * 128, 512:1024],
                      reads=[B("wd_wv")], writes=[wb1])
                with P.task(TE) as t:
                    for k in range(NQ):
                        ppo = pO if k == 0 else pF
                        ppob = ALL_pO if k == 0 else ALL_pF
                        t.op(lambda e, cin=cin, k=k, wsl=wsl, ppo=ppo:
                             e.matmul(
                                 ppo[:, 0:512],
                                 mix[:, cin * TCW + k * 128:
                                     cin * TCW + (k + 1) * 128],
                                 s_wvo[wsl][:, :],
                                 start=(cin == 0), stop=(cin == NCH - 1)),
                             reads=[mixb, wb0], writes=ppob,
                             waw=(cin == 0 and k == 0))
                        t.op(lambda e, cin=cin, k=k, wsl=wsl, ppo=ppo:
                             e.matmul(
                                 ppo[:, 512:1024],
                                 mix[:, cin * TCW + k * 128:
                                     cin * TCW + (k + 1) * 128],
                                 s_wvo[wsl + 1][:, :],
                                 start=(cin == 0), stop=(cin == NCH - 1)),
                             reads=[mixb, wb1], writes=ppob, waw=False)
            for k in range(NQ):
                ppo = pO if k == 0 else pF
                ppob = ALL_pO if k == 0 else ALL_pF
                with P.task(VE) as t:
                    t.op(lambda e, k=k, ppo=ppo: e.tensor_copy(
                        s_vtok[k][:, :], ppo[:, :]),
                        reads=ppob, writes=[B(f"vtok{k}")])

        def wkv_core():
            for cch in range(NCH):
                with P.task(VE) as t:
                    t.op(lambda e, cch=cch: e.tensor_mul(
                        s_rk[:, 0:TCW], s_rt[:, cch * TCW:(cch + 1) * TCW],
                        s_kt[:, cch * TCW:(cch + 1) * TCW]),
                        reads=[B("rtb"), B("ktb")], writes=[B("rk")])
                    t.op(lambda e, cch=cch: e.tensor_mul(
                        s_rk[:, 0:TCW], s_rk[:, 0:TCW],
                        s_emw[:, cch * TCW:(cch + 1) * TCW]),
                        reads=[B("rk"), B("emw")], writes=[B("rk")])
                with P.task(TE) as t:
                    t.op(lambda e, cch=cch: e.matmul(
                        pF[0:NH, 512:512 + TCW],
                        s_ub[:, cch * NH:(cch + 1) * NH], s_rk[:, 0:TCW],
                        start=(cch == 0), stop=(cch == NCH - 1)),
                        reads=[B("rk"), ST], writes=[bpF[2]], waw=(cch == 0))
            with P.task(VE) as t:
                t.op(lambda e: e.tensor_copy(s_g16[0:NH, 0:TCW],
                                             pF[0:NH, 512:512 + TCW]),
                     reads=[bpF[2]], writes=[B("g16")])
            for q in range(NQ):
                with P.task(TE) as t:
                    t.op(lambda e, q=q: e.transpose(
                        pF[:, 512:512 + NH], s_g16[0:NH, q * 128:(q + 1) * 128],
                        s_id32[0:NH, 0:NH]),
                        reads=[B("g16"), ST], writes=[bpF[2]])
                with P.task(VE) as t:
                    t.op(lambda e, q=q: e.tensor_copy(s_stat[:, 88 + q * 16:88 + (q + 1) * 16],
                                                      pF[:, 512:512 + NH]),
                         reads=[bpF[2]], writes=[B(f"dtok{q}")])
                wkv_qchunk(q)
                for cch in range(NCH):
                    ptile = pMb if cch % 2 == 0 else pNb
                    ptb = bM if cch % 2 == 0 else bN
                    with P.task(TE) as t:
                        t.op(lambda e, q=q, cch=cch, ptile=ptile: e.transpose(
                            ptile[:, 0:128],
                            s_ytok[:, cch * 128:(cch + 1) * 128],
                            s_ident[:]),
                            reads=[B("ytok"), ST], writes=[ptb])
                    with P.task(SC) as t:
                        t.op(lambda e, q=q, cch=cch, ptile=ptile:
                             e.copy(
                                 s_frw[:, cch * TCW + q * 128:
                                       cch * TCW + (q + 1) * 128],
                                 ptile[:, 0:128]),
                             reads=[ptb], writes=[B("frw")])

        def wkv_qchunk(q):
            with P.task(VE) as t:
                t.op(lambda e: e.tensor_copy(s_S16[:], s_S[:]),
                     reads=[B("wS")], writes=[B("wS16")])
            for h in range(NH):
                cch, half = h // 2, h % 2
                base = half * 64
                rsl = s_rt[base:base + 64,
                           cch * TCW + q * 128:cch * TCW + (q + 1) * 128]
                ksl = s_kt[base:base + 64,
                           cch * TCW + q * 128:cch * TCW + (q + 1) * 128]
                Ssl = s_S16[base:base + 64, cch * 64:cch * 64 + 64]
                psl = h % 2
                pAt = pM if psl == 0 else pN
                pA_reg = bM if psl == 0 else bN
                with P.task(TE) as t:
                    t.op(lambda e, ksl=ksl, rsl=rsl, pAt=pAt: e.matmul(
                        pAt[:, 0:128], ksl, rsl,
                        start=True, stop=True),
                        reads=[B("rtb"), B("ktb")], writes=[pA_reg])
                with P.task(VE) as t:
                    t.op(lambda e, psl=psl, pAt=pAt: e.tensor_mul(
                        s_AT[:, psl * 128:(psl + 1) * 128],
                        pAt[:, 0:128], s_mask[:]),
                        reads=[pA_reg, ST], writes=[B(f"AT{psl}")])
                yoff = (h % 2) * 512 + (h // 2) * 64
                yq = bO0 if h % 2 == 0 else bO1
                with P.task(TE) as t:
                    t.op(lambda e, rsl=rsl, Ssl=Ssl, yoff=yoff: e.matmul(
                        pO[:, yoff:yoff + 64], rsl, Ssl, start=True,
                        stop=False),
                        reads=[B("rtb"), B("wS16")], writes=[yq])
                    t.op(lambda e, psl=psl, yoff=yoff, h=h, q=q: e.matmul(
                        pO[:, yoff:yoff + 64],
                        s_AT[:, psl * 128:(psl + 1) * 128],
                        s_vtok[q][:, h * 64:(h + 1) * 64],
                        start=False, stop=True),
                        reads=[B(f"AT{psl}"), B(f"vtok{q}")], writes=[yq],
                        waw=False)
                with P.task(VE) as t:
                    t.op(lambda e, h=h, q=q, yoff=yoff:
                         e.scalar_tensor_tensor(
                             s_ytok[:, h * 64:h * 64 + 64],
                             s_vtok[q][:, h * 64:h * 64 + 64],
                             s_stat[:, 88 + q * 16 + h:88 + q * 16 + h + 1],
                             pO[:, yoff:yoff + 64],
                             ALU.mult, ALU.add),
                         reads=[B(f"vtok{q}"), B(f"dtok{q}"), yq],
                         writes=[B("ytok")])
                wb0 = psl * 64
                wtb = B(f"wt1{psl}")
                with P.task(VE) as t:
                    t.op(lambda e, ksl=ksl, wb0=wb0: e.tensor_copy(
                        s_wt1[wb0:wb0 + 64, 0:128], ksl),
                        reads=[B("ktb")], writes=[wtb])
                pTt = pY if psl == 0 else pB_
                pT_reg = bY if psl == 0 else bB
                with P.task(TE) as t:
                    t.op(lambda e, pTt=pTt, wb0=wb0: e.transpose(
                        pTt[:, 0:64],
                        s_wt1[wb0:wb0 + 64, 0:128],
                        s_id32[wb0:wb0 + 64, wb0:wb0 + 64]),
                        reads=[wtb, ST], writes=[pT_reg])
                with P.task(VE) as t:
                    t.op(lambda e, h=h, pTt=pTt: e.tensor_copy(
                        s_khT[:, (h % 2) * 64:(h % 2) * 64 + 64],
                        pTt[:, 0:64]),
                        reads=[pT_reg], writes=[B(f"khT{h % 2}")])
                kvoff = 0 if psl == 0 else 512
                kv_reg = bF0 if psl == 0 else bF1
                with P.task(TE) as t:
                    t.op(lambda e, h=h, q=q, kvoff=kvoff: e.matmul(
                        pF[0:64, kvoff:kvoff + 64],
                        s_khT[:, (h % 2) * 64:(h % 2) * 64 + 64],
                        s_vtok[q][:, h * 64:(h + 1) * 64],
                        start=True, stop=True),
                        reads=[B(f"khT{h % 2}"), B(f"vtok{q}")],
                        writes=[kv_reg])
                with P.task(VE) as t:
                    t.op(lambda e, kvoff=kvoff, base=base, cch=cch, wb0=wb0:
                         e.tensor_add(
                             s_wt1[wb0:wb0 + 64, 0:64],
                             s_S[base:base + 64, cch * 64:cch * 64 + 64],
                             pF[0:64, kvoff:kvoff + 64]),
                         reads=[B("wS"), kv_reg], writes=[wtb])
                    t.op(lambda e, q=q, base=base, cch=cch, wb0=wb0:
                         e.tensor_scalar_mul(
                             s_S[base:base + 64, cch * 64:cch * 64 + 64],
                             s_wt1[wb0:wb0 + 64, 0:64],
                             s_eQw[base:base + 64,
                                   cch * NQ + q:cch * NQ + q + 1]),
                         reads=[wtb, B("eQw")], writes=[B("wS")])

        def wkv_out(w, c):
            for cch in range(NCH):
                with P.task(SC) as t:
                    t.op(lambda e, cch=cch: e.activation(
                        s_wyn[:, :], s_frw[:, cch * TCW:(cch + 1) * TCW],
                        AF.Square),
                        reads=[B("frw")], writes=[B("wyn")])
                with P.task(TE) as t:
                    t.op(lambda e, cch=cch: e.matmul(
                        pF[0:NH, 0:TCW], s_gnsum[:, cch * NH:(cch + 1) * NH],
                        s_frw[:, cch * TCW:(cch + 1) * TCW],
                        start=(cch == 0), stop=(cch == NCH - 1)),
                        reads=[B("frw"), ST], writes=[bpF[0]],
                        waw=(cch == 0))
                    t.op(lambda e, cch=cch: e.matmul(
                        pF[0:NH, 512:512 + TCW],
                        s_gnsum[:, cch * NH:(cch + 1) * NH], s_wyn[:, :],
                        start=(cch == 0), stop=(cch == NCH - 1)),
                        reads=[B("wyn"), ST], writes=[bpF[2]],
                        waw=(cch == 0))
            with P.task(VE) as t:
                t.op(lambda e: e.tensor_scalar_mul(s_g16[:, 0:TCW],
                                                   pF[0:NH, 0:TCW], 1.0 / HD),
                     reads=[bpF[0]], writes=[B("g16")])
                t.op(lambda e: e.tensor_mul(s_g16[:, TCW:2 * TCW],
                                            s_g16[:, 0:TCW],
                                            s_g16[:, 0:TCW]),
                     reads=[B("g16")], writes=[B("g16")])
                t.op(lambda e: e.scalar_tensor_tensor(
                    s_g16[:, TCW:2 * TCW], pF[0:NH, 512:512 + TCW], 1.0 / HD,
                    s_g16[:, TCW:2 * TCW], ALU.mult, ALU.subtract),
                    reads=[bpF[2], B("g16")], writes=[B("g16")])
            with P.task(SC) as t:
                t.op(lambda e: e.activation(s_g16[:, TCW:2 * TCW],
                                            s_g16[:, TCW:2 * TCW], AF.Ln,
                                            bias=GN_EPS),
                     reads=[B("g16")], writes=[B("g16")])
                t.op(lambda e: e.activation(s_gbc[:, 0:TCW],
                                            s_g16[:, TCW:2 * TCW], AF.Exp,
                                            scale=-0.5),
                     reads=[B("g16")], writes=[B("gbc")])
            with P.task(VE) as t:
                t.op(lambda e: e.tensor_mul(s_gbc[:, TCW:2 * TCW],
                                            s_g16[:, 0:TCW],
                                            s_gbc[:, 0:TCW]),
                     reads=[B("g16"), B("gbc")], writes=[B("gbc")])
                t.op(lambda e: e.tensor_scalar_mul(s_gbc[:, TCW:2 * TCW],
                                                   s_gbc[:, TCW:2 * TCW],
                                                   -1.0),
                     reads=[B("gbc")], writes=[B("gbc")])
            for cch in range(NCH):
                with P.task(TE) as t:
                    t.op(lambda e, cch=cch: e.matmul(
                        pF[:, 0:TCW], s_gnind[:, cch * 128:(cch + 1) * 128],
                        s_gbc[:, 0:TCW], start=True, stop=True),
                        reads=[B("gbc"), ST], writes=[bpF[0]])
                    t.op(lambda e, cch=cch: e.matmul(
                        pF[:, 512:512 + TCW],
                        s_gnind[:, cch * 128:(cch + 1) * 128],
                        s_gbc[:, TCW:2 * TCW], start=True, stop=True),
                        reads=[B("gbc"), ST], writes=[bF1])
                with P.task(VE) as t:
                    t.op(lambda e, cch=cch: e.scalar_tensor_tensor(
                        s_wyn[:, :], pF[:, 0:TCW], vcol("lnw", cch),
                        s_frw[:, cch * TCW:(cch + 1) * TCW],
                        ALU.mult, ALU.mult),
                        reads=[B("frw"), bpF[0], ST], writes=[B("wyn")])
                    t.op(lambda e, cch=cch: e.scalar_tensor_tensor(
                        s_mix[:, cch * TCW:(cch + 1) * TCW],
                        pF[:, 512:512 + TCW], vcol("lnw", cch),
                        s_wyn[:, :], ALU.mult, ALU.add),
                        reads=[B("wyn"), bF1, ST], writes=[B("mix0")])
            tile0 = (w * TCM + c * TCW) // 128
            for cin in range(NCH):
                wsl = (2 * cin) % 4
                wb0, wb1 = B(f"wvo{wsl}"), B(f"wvo{wsl + 1}")
                P.dma(QS, s_wvo[wsl][:, :],
                      Wo_d[cin * 128:(cin + 1) * 128, 0:512],
                      reads=[B("wd_wo")], writes=[wb0])
                P.dma(QS, s_wvo[wsl + 1][:, :],
                      Wo_d[cin * 128:(cin + 1) * 128, 512:1024],
                      reads=[B("wd_wo")], writes=[wb1])
                with P.task(TE) as t:
                    for k in range(NQ):
                        ppo = pO if k == 0 else pF
                        ppob = ALL_pO if k == 0 else ALL_pF
                        if cin == 0:
                            t.op(lambda e, k=k, ppo=ppo: e.matmul(
                                ppo[:, 0:512], s_ones[0:1, :],
                                s_lnbWo[0:1, 0:512], start=True, stop=False),
                                reads=[ST], writes=ppob)
                            t.op(lambda e, k=k, ppo=ppo: e.matmul(
                                ppo[:, 512:1024], s_ones[0:1, :],
                                s_lnbWo[0:1, 512:1024], start=True,
                                stop=False),
                                reads=[ST], writes=ppob, waw=False)
                        t.op(lambda e, cin=cin, k=k, wsl=wsl, ppo=ppo:
                             e.matmul(
                                 ppo[:, 0:512],
                                 s_mix[:, cin * TCW + k * 128:
                                       cin * TCW + (k + 1) * 128],
                                 s_wvo[wsl][:, :], start=False,
                                 stop=(cin == NCH - 1)),
                             reads=[B("mix0"), wb0], writes=ppob, waw=False)
                        t.op(lambda e, cin=cin, k=k, wsl=wsl, ppo=ppo:
                             e.matmul(
                                 ppo[:, 512:1024],
                                 s_mix[:, cin * TCW + k * 128:
                                       cin * TCW + (k + 1) * 128],
                                 s_wvo[wsl + 1][:, :], start=False,
                                 stop=(cin == NCH - 1)),
                             reads=[B("mix0"), wb1], writes=ppob, waw=False)
            for k in range(NQ):
                tile = tile0 + k
                ppo = pO if k == 0 else pF
                ppob = ALL_pO if k == 0 else ALL_pF
                with P.task(VE) as t:
                    t.op(lambda e, tile=tile, ppo=ppo: e.tensor_add(
                        s_xres[xslot(tile)][:, :], s_xres[xslot(tile)][:, :],
                        ppo[:, :]),
                        reads=ppob + [xrb(tile)], writes=[xrb(tile)])

        def stage_W(w, c):
            tile0 = (w * TCM + c * TCW) // 128
            with P.task(GP) as t:
                for cch in range(NCH):
                    t.op(lambda e, cch=cch: e.tensor_copy(
                        x3f(cch)[:, 0:1], x3f(cch)[:, TCW:TCW + 1]),
                        reads=[B("x3f")], writes=[B("x3f")])
            for k in range(NQ):
                tile = tile0 + k
                xt = s_xtok[tile % 2]
                xtb = B(f"xtok{tile % 2}")
                ln_apply(tile, xt[:, :], xtb)
                for cch in range(NCH):
                    ptile = pMb if cch % 2 == 0 else pNb
                    ptb = bM if cch % 2 == 0 else bN
                    with P.task(TE) as t:
                        t.op(lambda e, cch=cch, ptile=ptile, xt=xt:
                             e.transpose(
                                 ptile[:, 0:128],
                                 xt[:, cch * 128:(cch + 1) * 128],
                                 s_ident[:]),
                             reads=[xtb, ST], writes=[ptb])
                    with P.task(SC) as t:
                        t.op(lambda e, cch=cch, k=k, ptile=ptile:
                             e.copy(
                                 x3f(cch)[:, 1 + k * 128:1 + (k + 1) * 128],
                                 ptile[:, 0:128]),
                             reads=[ptb], writes=[B("x3f")])
            with P.task(VE) as t:
                for cch in range(NCH):
                    t.op(lambda e, cch=cch: e.tensor_sub(
                        s_xx[:, cch * TCW:(cch + 1) * TCW],
                        x3f(cch)[:, 0:TCW], x3f(cch)[:, 1:TCW + 1]),
                        reads=[B("x3f")], writes=[B("xx")])
            for cch in range(NCH):
                with P.task(VE) as t:
                    t.op(lambda e, cch=cch: e.scalar_tensor_tensor(
                        s_wxx[:, :], s_xx[:, cch * TCW:(cch + 1) * TCW],
                        vcol("maax", cch), x3f(cch)[:, 1:TCW + 1],
                        ALU.mult, ALU.add),
                        reads=[B("xx"), B("x3f"), ST], writes=[B("wxx")])
                with P.task(TE) as t:
                    t.op(lambda e, cch=cch: e.matmul(
                        pF[:, 768:768 + TCW],
                        s_mw1[:, cch * 128:(cch + 1) * 128],
                        s_wxx[:, :], start=(cch == 0),
                        stop=(cch == NCH - 1)),
                        reads=[B("wxx"), ST], writes=[bpF[3]],
                        waw=(cch == 0))
            with P.task(SC) as t:
                for f in range(4):
                    t.op(lambda e, f=f: e.activation(
                        s_m1[:, f * TCW:(f + 1) * TCW],
                        pF[f * 32:(f + 1) * 32, 768:768 + TCW], AF.Tanh),
                        reads=[bpF[3]], writes=[B("m1")])
            for f in range(4):
                mixb = B("mix0")
                P.dma(QS, s_mw2s[f % 2][:, :], mw2_d[:, f * C:(f + 1) * C],
                      reads=[B("mw2d")], writes=[B(f"mw2s{f % 2}")])
                for cch in range(NCH):
                    with P.task(TE) as t:
                        t.op(lambda e, f=f, cch=cch: e.matmul(
                            pF[:, 768:768 + TCW],
                            s_mw2s[f % 2][:, cch * 128:(cch + 1) * 128],
                            s_m1[:, f * TCW:(f + 1) * TCW],
                            start=True, stop=True),
                            reads=[B("m1"), B(f"mw2s{f % 2}"), ST],
                            writes=[bpF[3]])
                    with P.task(VE) as t:
                        t.op(lambda e, f=f, cch=cch: e.scalar_tensor_tensor(
                            s_wt1[:, :], pF[:, 768:768 + TCW],
                            maacol(f, cch),
                            s_xx[:, cch * TCW:(cch + 1) * TCW],
                            ALU.add, ALU.mult),
                            reads=[bpF[3], B("xx"), ST], writes=[B("wt1")])
                        t.op(lambda e, f=f, cch=cch: e.tensor_add(
                            s_mix[:, cch * TCW:(cch + 1) * TCW], s_wt1[:, :],
                            x3f(cch)[:, 1:TCW + 1]),
                            reads=[B("wt1"), B("x3f")], writes=[mixb])
                if f == 0:
                    wkv_decay(mixb)
                elif f == 1:
                    wkv_proj(mixb, s_kt, B("ktb"), s_fkw, B("fkw"))
                elif f == 2:
                    wkv_vproj(mixb)
                else:
                    wkv_proj(mixb, s_rt, B("rtb"), s_frw, B("frw"))
            wkv_core()
            wkv_out(w, c)

        def store(w):
            t0 = w * NTT
            for k in range(NTT):
                tile = t0 + k
                P.dma(QA, y_out[tile * 128:(tile + 1) * 128, :],
                      s_xres[xslot(tile)][:, :], reads=[xrb(tile)])

        prep()
        for p in range(NW + 1):
            if p > 0:
                wp = p - 1
                if "2" in STAGES:
                    for pr in range(F_PAIRS):
                        stage_Bq(wp, pr)
                        stage_F(wp, pr)
                if "W" in STAGES:
                    for c in range(NWC):
                        stage_W(wp, c)
                if "4" in STAGES:
                    for pr in range(2):
                        stage_Bq(wp, pr)
                        stage_F(wp, pr)
                store(wp)
            if p < NW and "A" in STAGES:
                stage_A(p)
        for slot, cntv in P.scnt.items():
            stream = SC if slot.startswith("o_xres") or slot.startswith("bcrow") \
                else SY
            P.thunks[stream].append(
                lambda e, s=P.sems[slot], v=cntv: e.wait_ge(s, v))

        block = stack.enter_context(nc.Block())
        for e, attr in ((TE, "tensor"), (VE, "vector"), (SC, "scalar"),
                        (GP, "gpsimd"), (SY, "sync")):
            thunks = P.thunks[e]

            def body(eng, thunks=thunks):
                for f in thunks:
                    f(eng)
            getattr(block, attr)(body)

    return nc


_CACHE = {}


def kernel(**inputs):
    x = np.ascontiguousarray(inputs["L0_attn_a_x"], dtype=np.float32)
    Bn, T, Cd = x.shape
    if T not in _CACHE:
        _CACHE[T] = build_program(T)
    nc = _CACHE[T]
    wmap = {
        "conv_kernel": "L0_attn_a_conv_kernel", "A_log": "L0_attn_a_A_log",
        "W_bc": "L0_attn_a_W_bc", "b_bc": "L0_attn_a_b_bc",
        "Dcoeff": "L0_attn_a_Dcoeff", "W_dt": "L0_attn_a_W_dt",
        "b_dt": "L0_attn_a_b_dt", "W_dtp": "L0_attn_a_W_dtp",
        "b_dtp": "L0_attn_a_b_dtp",
        "time_maa_x": "L0_attn_b_time_maa_x",
        "time_maa_w": "L0_attn_b_time_maa_w",
        "time_maa_k": "L0_attn_b_time_maa_k",
        "time_maa_v": "L0_attn_b_time_maa_v",
        "time_maa_r": "L0_attn_b_time_maa_r",
        "time_maa_w1": "L0_attn_b_time_maa_w1",
        "time_maa_w2": "L0_attn_b_time_maa_w2",
        "time_decay": "L0_attn_b_time_decay",
        "time_decay_w1": "L0_attn_b_time_decay_w1",
        "time_decay_w2": "L0_attn_b_time_decay_w2",
        "time_faaaa": "L0_attn_b_time_faaaa", "W_r": "L0_attn_b_W_r",
        "W_k": "L0_attn_b_W_k", "W_v": "L0_attn_b_W_v",
        "W_o": "L0_attn_b_W_o",
        "ln_w": "L0_attn_b_ln_w", "ln_b": "L0_attn_b_ln_b",
        "ffn_w1": "L0_ffn_w1", "ffn_b1": "L0_ffn_b1",
        "ffn_w2": "L0_ffn_w2", "ffn_b2": "L0_ffn_b2",
    }
    base = {k: np.ascontiguousarray(inputs[v], dtype=np.float32)
            for k, v in wmap.items()}
    in_maps = []
    for i in range(Bn):
        m = dict(base)
        m["x"] = np.ascontiguousarray(x[i])
        in_maps.append(m)
    res = run_bass_kernel_spmd(nc, in_maps, core_ids=list(range(Bn)))
    out = np.stack([r["y"] for r in res.results], axis=0)
    return out.astype(np.float32)
